# version 09
# baseline (speedup 1.0000x reference)
import numpy as np
import concourse.bass as bass
import concourse.mybir as mybir
from concourse.bass_utils import run_bass_kernel_spmd
from contextlib import ExitStack

F32 = mybir.dt.float32
BF16 = mybir.dt.bfloat16
I32 = mybir.dt.int32
AF = mybir.ActivationFunctionType
ALU = mybir.AluOpType
AX = mybir.AxisListType

D = 1024
S = 4096
NCORES = 8
MAGIC = 0x5F3759DF
EPS = 1e-5
AF_COPY = AF.Identity


class _Sem:
    __slots__ = ("h", "step", "n")

    def __init__(self, h, step):
        self.h = h
        self.step = step
        self.n = 0


class _Stream:
    __slots__ = ("e", "csem", "dsems", "drot", "waited")

    def __init__(self, e, csem, dsems):
        self.e = e
        self.csem = csem
        self.dsems = dsems
        self.drot = 0
        self.waited = {}


class _B:

    def __init__(self, nc, ctx):
        self.nc = nc
        self.ctx = ctx
        self.st = {}
        for name, eng, n_d in (
            ("pe", nc.tensor, 0),
            ("act", nc.scalar, 0),
            ("dve", nc.vector, 0),
            ("pool", nc.gpsimd, 16),
            ("sq", nc.sync, 6),
        ):
            cs = _Sem(ctx.enter_context(nc.semaphore(name + "_c")), 1)
            ds = [_Sem(ctx.enter_context(nc.semaphore(f"{name}_d{i}")), 16)
                  for i in range(n_d)]
            self.st[name] = _Stream(eng, cs, ds)

    def emit(self, stname, fn, deps=(), dma=False):
        st = self.st[stname]
        for dep in deps:
            if dep is None:
                continue
            sem, cnt = dep
            if st.waited.get(id(sem), 0) >= cnt:
                continue
            st.e.wait_ge(sem.h, cnt * sem.step)
            st.waited[id(sem)] = cnt
        if dma:
            sem = st.dsems[st.drot]
            st.drot = (st.drot + 1) % len(st.dsems)
        else:
            sem = st.csem
        h = fn(st.e)
        h.then_inc(sem.h, sem.step)
        sem.n += 1
        return (sem, sem.n)

    def wait(self, stname, deps):
        st = self.st[stname]
        for dep in deps:
            if dep is None:
                continue
            sem, cnt = dep
            if st.waited.get(id(sem), 0) >= cnt:
                continue
            st.e.wait_ge(sem.h, cnt * sem.step)
            st.waited[id(sem)] = cnt

    def snapshot(self):
        toks = []
        for st in self.st.values():
            if st.csem.n:
                toks.append((st.csem, st.csem.n))
            for ds in st.dsems:
                if ds.n:
                    toks.append((ds, ds.n))
        return toks

    def barrier(self, toks):
        for name, st in self.st.items():
            own = {id(st.csem)} | {id(d) for d in st.dsems}
            for sem, cnt in toks:
                if id(sem) in own:
                    continue
                self.wait(name, [(sem, cnt)])


def _re3(ap, k=128):
    return ap.rearrange("p (c k) -> p c k", k=k)


def build_program(nt, debug=False):
    own = (nt - 1) * 128
    rows = nt * 128
    nc = bass.Bass("TRN2", target_bir_lowering=False)
    dbg = {}
    if debug:
        dbg["dbgB"] = nc.dram_tensor("dbgB", [8 * 128, D], BF16, kind="ExternalOutput").ap()
        dbg["dbgF"] = nc.dram_tensor("dbgF", [2 * 128, D], F32, kind="ExternalOutput").ap()
        dbg["dbgS"] = nc.dram_tensor("dbgS", [128, 32], F32, kind="ExternalOutput").ap()

    xin = nc.dram_tensor("xin", [rows, D], F32, kind="ExternalInput").ap()
    kT = nc.dram_tensor("kT", [1024, 4096], F32, kind="ExternalInput").ap()
    vv = nc.dram_tensor("vv", [1024, 4096], F32, kind="ExternalInput").ap()
    wrT = nc.dram_tensor("wrT", [256, 4096], F32, kind="ExternalInput").ap()
    wkT = nc.dram_tensor("wkT", [256, 4096], F32, kind="ExternalInput").ap()
    wvT = nc.dram_tensor("wvT", [256, 4096], F32, kind="ExternalInput").ap()
    mur = nc.dram_tensor("mur", [128, 8], F32, kind="ExternalInput").ap()
    muk = nc.dram_tensor("muk", [128, 8], F32, kind="ExternalInput").ap()
    msk = nc.dram_tensor("msk", [128, 1], F32, kind="ExternalInput").ap()
    out = nc.dram_tensor("out", [own, D], F32, kind="ExternalOutput").ap()
    x1s = nc.dram_tensor("x1s", [rows, D], F32,
                         kind="ExternalOutput" if debug else "Internal").ap()

    with ExitStack() as ctx:
        b = _B(nc, ctx)
        sb = lambda n, s, d: ctx.enter_context(nc.sbuf_tensor(n, s, d)).ap()
        pst = lambda n, s, d: ctx.enter_context(nc.psum_tensor(n, s, d)).ap()

        ident = sb("ident", [128, 128], BF16)
        magic = sb("magic", [128, 1], I32)
        mur_sb = sb("mur_sb", [128, 8], F32)
        muk_sb = sb("muk_sb", [128, 8], F32)
        msk_sb = sb("msk_sb", [128, 1], F32)

        t_id0 = b.emit("pool", lambda e: e.memset(ident, 0.0))
        t_id = b.emit(
            "pool",
            lambda e: e.affine_select(
                out=ident, in_=ident, compare_op=ALU.not_equal, fill=1.0,
                base=0, pattern=[[-1, 128]], channel_multiplier=1),
            deps=[t_id0],
        )
        t_magic = b.emit("pool", lambda e: e.memset(magic, MAGIC))
        t_mur = b.emit("sq", lambda e: e.dma_start(out=mur_sb, in_=mur), dma=True)
        t_muk = b.emit("sq", lambda e: e.dma_start(out=muk_sb, in_=muk), dma=True)
        t_msk = b.emit("sq", lambda e: e.dma_start(out=msk_sb, in_=msk), dma=True)

        def rsqrt(stt, base, v_ap, vtok):
            c0 = stt[:, base:base + 1]
            y0 = stt[:, base + 1:base + 2]
            t1 = stt[:, base + 2:base + 3]
            t2 = stt[:, base + 3:base + 4]
            yf = stt[:, base + 4:base + 5]
            tk = b.emit("dve", lambda e: e.tensor_scalar(
                out=c0.bitcast(I32), in0=v_ap.bitcast(I32), scalar1=1,
                scalar2=None, op0=ALU.logical_shift_right), deps=[vtok])
            tk = b.emit("pool", lambda e: e.tensor_tensor(
                out=y0.bitcast(I32), in0=magic, in1=c0.bitcast(I32),
                op=ALU.subtract), deps=[tk, t_magic])
            tk = b.emit("dve", lambda e: e.tensor_tensor(out=t1, in0=y0, in1=y0, op=ALU.mult), deps=[tk])
            tk = b.emit("dve", lambda e: e.tensor_tensor(out=t2, in0=t1, in1=v_ap, op=ALU.mult), deps=[tk])
            tk = b.emit("dve", lambda e: e.tensor_scalar(out=t1, in0=t2, scalar1=-0.5, scalar2=1.5, op0=ALU.mult, op1=ALU.add), deps=[tk])
            tk = b.emit("dve", lambda e: e.tensor_tensor(out=yf, in0=y0, in1=t1, op=ALU.mult), deps=[tk])
            tk = b.emit("dve", lambda e: e.tensor_tensor(out=t2, in0=yf, in1=yf, op=ALU.mult), deps=[tk])
            tk = b.emit("dve", lambda e: e.tensor_tensor(out=t1, in0=t2, in1=v_ap, op=ALU.mult), deps=[tk])
            tk = b.emit("dve", lambda e: e.tensor_scalar(out=t2, in0=t1, scalar1=-0.5, scalar2=1.5, op0=ALU.mult, op1=ALU.add), deps=[tk])
            tk = b.emit("dve", lambda e: e.tensor_tensor(out=y0, in0=yf, in1=t2, op=ALU.mult), deps=[tk])
            return tk, y0

        hmap = {}
        with ExitStack() as actx:
            def ctx_sb(c, n, s, d):
                h = c.enter_context(nc.sbuf_tensor(n, s, d))
                hmap[n] = h
                return h.ap()

            def ctx_ps(c, n, s, d):
                return c.enter_context(nc.psum_tensor(n, s, d)).ap()

            K_sb = ctx_sb(actx, "K_sb", [128, 8 * S], BF16)
            V_sb = ctx_sb(actx, "V_sb", [128, 32 * D], BF16)
            xb = [ctx_sb(actx, f"xb{i}", [128, D], F32) for i in range(3)]
            x1b = ctx_sb(actx, "x1b", [128, D], F32)
            ln1b = [ctx_sb(actx, f"ln1b{i}", [128, D], BF16) for i in range(2)]
            ln1T = [ctx_sb(actx, f"ln1T{i}", [128, D], BF16) for i in range(2)]
            gbf = [ctx_sb(actx, f"gbf{i}", [128, 512], BF16) for i in range(2)]
            gTb = [ctx_sb(actx, f"gTb{i}", [128, 512], BF16) for i in range(2)]
            ao = ctx_sb(actx, "ao", [128, D], F32)
            scr = ctx_sb(actx, "scr", [128, D], F32)
            junk = ctx_sb(actx, "junk", [128, D], F32)
            stats = [ctx_sb(actx, f"stats{i}", [128, 32], F32) for i in range(2)]

            ps_s = [ctx_ps(actx, f"ps_s{i}", [128, 512], F32) for i in range(2)]
            ps_t = [ctx_ps(actx, f"ps_t{i}", [128, 1024], BF16) for i in range(2)]
            ps_lt = [ctx_ps(actx, f"ps_lt{i}", [128, 1024], BF16) for i in range(2)]
            ps_o = ctx_ps(actx, "ps_o", [128, 1024], F32)

            t_kdma = [None] * 8
            t_vdma = [None] * 8
            for j in range(8):
                t_kdma[j] = b.emit(
                    "pool",
                    lambda e, j=j: e.dma_start(
                        out=K_sb[:, j * 4096:(j + 1) * 4096],
                        in_=kT[j * 128:(j + 1) * 128, :]),
                    dma=True)
                t_vdma[j] = b.emit(
                    "pool", lambda e, j=j: e.dma_start(
                        out=V_sb[:, j * 4096:(j + 1) * 4096],
                        in_=vv[j * 128:(j + 1) * 128, :]),
                    dma=True)

            t_ld = {}

            def emit_load(t, dep):
                if t >= nt:
                    return
                p3 = t % 3
                t_ld[t] = b.emit(
                    "sq",
                    lambda e: e.dma_start(out=xb[p3], in_=xin[t * 128:(t + 1) * 128, :]),
                    deps=[dep], dma=True)

            emit_load(0, None)
            emit_load(1, None)
            emit_load(2, None)

            tok = {
                "ln1b_rd": [None, None],
                "ln1T_rd": [None, None],
                "pslt_rd": [None, None],
                "pss_rd": [None, None],
                "gbf_rd": [None, None],
                "pst_rd": [None, None],
                "gTb_rd": [None, None],
                "ao_wr": None,
                "ao_dve_rd": None,
                "tmp_wr": None,
                "x1add": {},
                "store": None,
                "store_by_tile": {},
                "mm2_last": None,
                "last_mm1": None,
                "stats_rd_act": [None, None],
            }

            pdve = {}

            def pass_a_P_dve(t):
                p = t % 2
                p3 = t % 3
                stt = stats[p]
                d_x = t_ld[t]
                tk_r = b.emit("dve", lambda e: e.reduce_sum(
                    out=stt[:, 0:1], in_=xb[p3], axis=AX.X), deps=[d_x])
                tk_m = b.emit("dve", lambda e: e.tensor_scalar(
                    out=stt[:, 1:2], in0=stt[:, 0:1], scalar1=1.0 / D,
                    scalar2=None, op0=ALU.mult), deps=[tk_r])
                tk_v = b.emit("dve", lambda e: e.scalar_tensor_tensor(
                    out=junk, in0=xb[p3], scalar=stt[:, 1:2], in1=xb[p3],
                    op0=ALU.subtract, op1=ALU.mult, accum_out=stt[:, 2:3]),
                    deps=[tk_m, tok.get("junk_wr")])
                tok["junk_wr"] = tk_v
                tk_ve = b.emit("dve", lambda e: e.tensor_scalar(
                    out=stt[:, 3:4], in0=stt[:, 2:3], scalar1=1.0 / D,
                    scalar2=EPS, op0=ALU.mult, op1=ALU.add), deps=[tk_v])
                tk_rs, rstd = rsqrt(stt, 4, stt[:, 3:4], tk_ve)
                tk_b = b.emit("dve", lambda e: e.scalar_tensor_tensor(
                    out=stt[:, 9:10], in0=stt[:, 1:2], scalar=-1.0, in1=rstd,
                    op0=ALU.mult, op1=ALU.mult), deps=[tk_rs])
                pdve[t] = (tk_b, rstd)

            def pass_a_P_act(t):
                p = t % 2
                p3 = t % 3
                stt = stats[p]
                tk_b, rstd = pdve[t]
                t_n = b.emit("act", lambda e: e.activation(
                    out=ln1b[p], in_=xb[p3], func=AF_COPY,
                    bias=stt[:, 9:10], scale=rstd),
                    deps=[tk_b, tok["ln1b_rd"][p]])
                tt = None
                for ci in range(8):
                    dl = [t_n, tok["pslt_rd"][p]] if ci == 0 else []
                    tt = b.emit("pe", lambda e, ci=ci: e.transpose(
                        ps_lt[p][:, ci * 128:(ci + 1) * 128],
                        ln1b[p][:, ci * 128:(ci + 1) * 128], ident),
                        deps=dl + ([t_id] if ci == 0 else []))
                tok["ln1b_rd"][p] = tt
                ev = b.emit("act", lambda e: e.copy(ln1T[p], ps_lt[p]),
                            deps=[tt, tok["ln1T_rd"][p]])
                tok["pslt_rd"][p] = ev
                return ev

            def pass_a_C(t, ev_ln1T):
                p = t % 2
                stt = stats[p]
                mm1 = {}
                gel = {}
                g2tok = {}
                tr = {}
                evg = {}
                mm2tok = None

                def do_mm1(j):
                    par = j % 2
                    tk = None
                    for d in range(8):
                        dl = []
                        if d == 0:
                            dl = [ev_ln1T, t_kdma[j], tok["pss_rd"][par]]
                        tk = b.emit("pe", lambda e, d=d, j=j, par=par: e.matmul(
                            ps_s[par],
                            ln1T[p][:, d * 128:(d + 1) * 128],
                            K_sb[:, j * 4096 + d * 512: j * 4096 + (d + 1) * 512],
                            start=(d == 0), stop=(d == 7)), deps=dl)
                    mm1[j] = tk

                def do_gelu(j):
                    par = j % 2
                    gel[j] = b.emit("act", lambda e, par=par: e.activation(
                        out=gbf[par], in_=ps_s[par], func=AF.Gelu),
                        deps=[mm1[j]] + list(tok["gbf_rd"][par] or ()))
                    tok["pss_rd"][par] = gel[j]

                def do_g2(j):
                    par = j % 2
                    g2tok[j] = b.emit("dve", lambda e, j=j, par=par: e.scalar_tensor_tensor(
                        out=junk[:, 0:512], in0=gbf[par], scalar=0.0,
                        in1=gbf[par], op0=ALU.bypass, op1=ALU.mult,
                        accum_out=stt[:, 21 + j:22 + j]),
                        deps=[gel[j], tok.get("junk_wr")])
                    tok["junk_wr"] = g2tok[j]

                def do_tr(j):
                    par = j % 2
                    tk = None
                    for q in range(4):
                        dl = [gel[j], tok["pst_rd"][par]] if q == 0 else []
                        tk = b.emit("pe", lambda e, q=q, par=par: e.transpose(
                            ps_t[par][:, q * 128:(q + 1) * 128],
                            gbf[par][:, q * 128:(q + 1) * 128], ident), deps=dl)
                    tr[j] = tk

                def do_evg(j):
                    par = j % 2
                    evg[j] = b.emit("act", lambda e, par=par: e.copy(
                        gTb[par], ps_t[par][:, 0:512]),
                        deps=[tr[j], tok["gTb_rd"][par]])
                    tok["pst_rd"][par] = evg[j]

                def do_mm2(j):
                    nonlocal mm2tok
                    par = j % 2
                    tk = None
                    for q in range(4):
                        for h in range(2):
                            dl = []
                            if q == 0 and h == 0:
                                dl = [evg[j], t_vdma[j], tok["ao_wr"]]
                            st_flag = (j == 0 and q == 0)
                            sp_flag = (j == 7 and q == 3)
                            tk = b.emit("pe", lambda e, j=j, q=q, h=h, par=par,
                                        st_flag=st_flag, sp_flag=sp_flag: e.matmul(
                                ps_o[:, h * 512:(h + 1) * 512],
                                gTb[par][:, q * 128:(q + 1) * 128],
                                V_sb[:, (4 * j + q) * D + h * 512:(4 * j + q) * D + (h + 1) * 512],
                                start=st_flag, stop=sp_flag), deps=dl)
                    mm2[j] = tk
                    mm2tok = tk

                mm2 = {}
                do_mm1(0)
                do_gelu(0)
                do_mm1(1)
                do_gelu(1)
                if t + 1 < nt:
                    pass_a_P_dve(t + 1)
                do_g2(0)
                do_g2(1)
                if t > 0:
                    pass_a_E_rest(t - 1, g2_of[t - 1], ao_of[t - 1])
                    emit_load(t + 2, tok["x1add"].get(t - 1))
                for j in range(8):
                    do_tr(j)
                    do_evg(j)
                    tok["gbf_rd"][j % 2] = (tr[j], g2tok[j])
                    if j + 2 < 8:
                        do_mm1(j + 2)
                    do_mm2(j)
                    tok["gTb_rd"][j % 2] = mm2[j]
                    if j + 2 < 8:
                        do_gelu(j + 2)
                        do_g2(j + 2)
                tok["mm2_last"] = mm2tok
                tok["ln1T_rd"][p] = mm1[7]
                tok["last_mm1"] = mm1[7]
                ev_next = pass_a_P_act(t + 1) if t + 1 < nt else None
                return g2tok[7], mm2tok, ev_next

            def pass_a_ao(t, mm2tok):
                t_ao = b.emit("act", lambda e: e.copy(ao, ps_o),
                              deps=[mm2tok, tok["ao_dve_rd"]])
                tok["ao_wr"] = t_ao
                return t_ao

            def pass_a_E_rest(t, g2last, t_ao):
                p = t % 2
                p3 = t % 3
                stt = stats[p]
                t_sg = b.emit("dve", lambda e: e.reduce_sum(
                    out=stt[:, 29:30], in_=stt[:, 21:29], axis=AX.X),
                    deps=[g2last])
                tk_r = b.emit("dve", lambda e: e.reduce_sum(
                    out=stt[:, 10:11], in_=ao, axis=AX.X), deps=[t_ao])
                tk_m = b.emit("dve", lambda e: e.tensor_scalar(
                    out=stt[:, 11:12], in0=stt[:, 10:11], scalar1=1.0 / D,
                    scalar2=None, op0=ALU.mult), deps=[tk_r])
                tk_v = b.emit("dve", lambda e: e.scalar_tensor_tensor(
                    out=junk, in0=ao, scalar=stt[:, 11:12], in1=ao,
                    op0=ALU.subtract, op1=ALU.mult, accum_out=stt[:, 12:13]),
                    deps=[tk_m, tok.get("junk_wr")])
                tok["junk_wr"] = tk_v
                tok["ao_dve_rd"] = tk_v
                tk_vs = b.emit("dve", lambda e: e.tensor_scalar(
                    out=stt[:, 13:14], in0=stt[:, 12:13], scalar1=1.0 / D,
                    scalar2=None, op0=ALU.mult), deps=[tk_v])
                tk_ve = b.emit("dve", lambda e: e.scalar_tensor_tensor(
                    out=stt[:, 14:15], in0=stt[:, 29:30], scalar=EPS / S,
                    in1=stt[:, 13:14], op0=ALU.mult, op1=ALU.add),
                    deps=[tk_vs, t_sg])
                tk_rs, rstd = rsqrt(stt, 15, stt[:, 14:15], tk_ve)
                t_tmp = b.emit("dve", lambda e: e.tensor_scalar(
                    out=scr, in0=ao, scalar1=stt[:, 11:12], scalar2=rstd,
                    op0=ALU.subtract, op1=ALU.mult),
                    deps=[tk_rs, tok.get("x1add_prev_scr")])
                t_x1 = b.emit("dve", lambda e: e.tensor_tensor(
                    out=x1b, in0=xb[p3], in1=scr, op=ALU.add),
                    deps=[t_tmp, tok["store"]])
                tok["x1add_prev_scr"] = t_x1
                tok["x1add"][t] = t_x1
                tok["store"] = b.emit("sq", lambda e: e.dma_start(
                    out=x1s[t * 128:(t + 1) * 128, :], in_=x1b),
                    deps=[t_x1], dma=True)
                tok["store_by_tile"][t] = tok["store"]

            tok["x1add_prev_scr"] = None
            pass_a_P_dve(0)
            ev_next = pass_a_P_act(0)
            g2_of = {}
            ao_of = {}
            mm2_prev = None
            for t in range(nt):
                if t > 0:
                    ao_of[t - 1] = pass_a_ao(t - 1, mm2_prev)
                g2_of[t], mm2_prev, ev_next = pass_a_C(t, ev_next)
            ao_of[nt - 1] = pass_a_ao(nt - 1, mm2_prev)
            pass_a_E_rest(nt - 1, g2_of[nt - 1], ao_of[nt - 1])

            bar = b.snapshot()

        def L(*xs):
            out_ = []
            for x in xs:
                if x is None:
                    continue
                if isinstance(x, tuple) and len(x) == 2 and isinstance(x[0], _Sem):
                    out_.append(x)
                elif isinstance(x, (list, tuple)):
                    out_.extend(L(*x))
                else:
                    out_.append(x)
            return out_

        with ExitStack() as bctx:
            hB = {}

            def bsb(n, s, d):
                h = bctx.enter_context(nc.sbuf_tensor(n, s, d))
                hB[n] = h
                return h.ap()

            def bps(n, s, d):
                return bctx.enter_context(nc.psum_tensor(n, s, d)).ap()

            Wk_sb = bsb("Wk_sb", [128, 8 * D], BF16)
            Wv_sb = bsb("Wv_sb", [128, 8 * D], BF16)
            Wr_sb = bsb("Wr_sb", [128, 8 * D], BF16)
            y1b = [bsb(f"y1b{i}", [128, D], F32) for i in range(4)]
            k_addr = nc.lookup_mloc(hmap["K_sb"]).addr
            assert nc.lookup_mloc(hB["Wk_sb"]).addr == k_addr, "alloc layout"
            assert nc.lookup_mloc(hB["y1b3"]).addr + 4096 <= k_addr + 65536, \
                "y1 ring must stay inside K_sb's span"

            lm = tok["last_mm1"]
            t_wdma = {"r": [], "k": [], "v": []}
            for nm, dram, dst in (("k", wkT, Wk_sb), ("v", wvT, Wv_sb),
                                  ("r", wrT, Wr_sb)):
                for h in range(2):
                    t_wdma[nm].append(b.emit(
                        "pool",
                        lambda e, dram=dram, dst=dst, h=h: e.dma_start(
                            out=dst[:, h * 4096:(h + 1) * 4096],
                            in_=dram[h * 128:(h + 1) * 128, :]),
                        deps=[lm], dma=True))

            t_yld = {}

            def emit_yload(t, extra):
                if t >= nt:
                    return
                t_yld[t] = b.emit(
                    "sq",
                    lambda e, t=t: e.dma_start(
                        out=y1b[t % 4], in_=x1s[t * 128:(t + 1) * 128, :]),
                    deps=L(extra), dma=True)

            b.barrier(bar)
            b.wait("sq", [(ds, ds.n) for ds in b.st["sq"].dsems if ds.n])

            for t0 in range(4):
                emit_yload(t0, None)

            ln2b = [bsb(f"ln2b{i}", [128, D], BF16) for i in range(2)]
            ln2T = [bsb(f"ln2T{i}", [128, D], BF16) for i in range(2)]
            xrT = [bsb(f"xrT{i}", [128, D], BF16) for i in range(2)]
            xkT = [bsb(f"xkT{i}", [128, D], BF16) for i in range(2)]
            mixd = bsb("mixd", [128, D], BF16)
            muMr = bsb("muMr", [128, D], BF16)
            muMk = bsb("muMk", [128, D], BF16)
            tmpb = bsb("tmpb", [128, D], BF16)
            tmpk = bsb("tmpk", [128, D], BF16)
            t_rb = [bsb(f"t_rb{i}", [128, D], F32) for i in range(2)]
            rkb = bsb("rkb", [128, D], BF16)
            kctb = [bsb(f"kctb{i}", [128, D], BF16) for i in range(2)]
            kctT = bsb("kctT", [128, D], BF16)
            yb = bsb("yb", [128, D], F32)
            scrB = bsb("scrB", [128, D], F32)
            outb = [bsb(f"outb{i}", [128, D], F32) for i in range(2)]
            stL2 = [bsb(f"stL2{i}", [128, 32], F32) for i in range(2)]
            stL4 = [bsb(f"stL4{i}", [128, 32], F32) for i in range(2)]
            prevc = [bsb(f"prevc{i}", [128, 8], BF16) for i in range(2)]
            tmp8 = bsb("tmp8", [128, 8], BF16)

            ps_r = bps("ps_r", [128, 1024], F32)
            ps_k = bps("ps_k", [128, 1024], F32)
            ps_v = bps("ps_v", [128, 1024], F32)
            ps_t1 = bps("ps_tb1", [128, 1024], BF16)
            ps_t2 = bps("ps_tb2", [128, 1024], BF16)

            t_z0 = b.emit("pool", lambda e: e.memset(muMr, 0.0))
            t_z1 = b.emit("pool", lambda e: e.memset(muMk, 0.0))
            t_mm = {}
            for nm, dst, src, tz, tmu in (("r", muMr, mur_sb, t_z0, t_mur),
                                          ("k", muMk, muk_sb, t_z1, t_muk)):
                tk = None
                for ci in range(8):
                    dl = [tz, tmu] if ci == 0 else []
                    tk = b.emit("pool", lambda e, dst=dst, src=src, ci=ci: e.tensor_scalar(
                        out=dst[:, ci * 128:(ci + 1) * 128],
                        in0=dst[:, ci * 128:(ci + 1) * 128],
                        scalar1=src[:, ci:ci + 1], scalar2=-1.0,
                        op0=ALU.add, op1=ALU.add), deps=dl)
                t_mm[nm] = tk
            t_pc0 = b.emit("pool", lambda e: e.memset(prevc[0], 0.0))
            t_pc1 = b.emit("pool", lambda e: e.memset(prevc[1], 0.0))

            tkB = {
                "ln2b_rd": [None, None],
                "ln2b_wr": {},
                "ln2T_rd": [None, None],
                "ps_t1_rd": None, "ps_t2_rd": None,
                "mix_done": {},
                "mixd_rd": None,
                "xrT_rd": [None, None], "xkT_rd": [None, None],
                "tanh": {}, "relu": {}, "square": {},
                "trb_rd": [None, None],
                "kctb_rd": [None, None],
                "kctT_rd": None,
                "y_tok": {},
                "yb_rd": None, "scr_rd": None,
                "outb_st": [None, None],
                "prevc_rd": [t_pc0, t_pc1],
                "prevc_wr": [t_pc0, t_pc1],
                "stL2_free": [None, None],
                "stL4_free": [None, None],
            }

            def ln2_stats(t):
                p = t % 2
                st = stL2[p]
                b1 = b.emit("dve", lambda e, st=st, t=t: e.bn_stats(
                    st[:, 0:6], y1b[t % 4][:, 0:512]),
                    deps=L(t_yld[t], tkB["stL2_free"][p]))
                b2 = b.emit("dve", lambda e, st=st, t=t: e.bn_stats(
                    st[:, 6:12], y1b[t % 4][:, 512:1024]), deps=[b1])
                ag = b.emit("dve", lambda e, st=st: e.bn_aggr(
                    st[:, 12:14], st[:, 0:12]), deps=[b2])
                ve = b.emit("dve", lambda e, st=st: e.tensor_scalar(
                    out=st[:, 14:15], in0=st[:, 13:14], scalar1=EPS,
                    scalar2=None, op0=ALU.add), deps=[ag])
                tk_rs, rstd = rsqrt(st, 15, st[:, 14:15], ve)
                tk_b = b.emit("dve", lambda e, st=st: e.scalar_tensor_tensor(
                    out=st[:, 20:21], in0=st[:, 12:13], scalar=-1.0, in1=rstd,
                    op0=ALU.mult, op1=ALU.mult), deps=[tk_rs])
                t_n = b.emit("act", lambda e, st=st, p=p, t=t, rstd=rstd: e.activation(
                    out=ln2b[p], in_=y1b[t % 4], func=AF_COPY,
                    bias=st[:, 20:21], scale=rstd),
                    deps=L(tk_b, tkB["ln2b_rd"][p]))
                tkB["stL2_free"][p] = t_n
                tkB["ln2b_wr"][t] = t_n

            def tr_mix(A):
                p = A % 2
                tt = None
                for ci in range(8):
                    dl = L(tkB["ln2b_wr"][A], tkB["ps_t1_rd"], t_id) if ci == 0 else []
                    tt = b.emit("pe", lambda e, ci=ci, p=p: e.transpose(
                        ps_t1[:, ci * 128:(ci + 1) * 128],
                        ln2b[p][:, ci * 128:(ci + 1) * 128], ident), deps=dl)
                tkB["ln2b_rd"][p] = tt
                ev = b.emit("act", lambda e, p=p: e.copy(ln2T[p], ps_t1),
                            deps=L(tt, tkB["ln2T_rd"][p]))
                tkB["ps_t1_rd"] = ev
                l3 = _re3(ln2T[p])
                m3 = _re3(mixd)
                pcw = b.emit("dve", lambda e, p=p: e.tensor_copy(
                    out=_re3(prevc[p], k=1), in_=_re3(ln2T[p])[:, :, 127:128]),
                    deps=L(ev, tkB["prevc_rd"][p]))
                tkB["prevc_wr"][p] = pcw
                pp = (A - 1) % 2
                t_bulk = b.emit("dve", lambda e, p=p: e.tensor_tensor(
                    out=_re3(mixd)[:, :, 1:128], in0=_re3(ln2T[p])[:, :, 1:128],
                    in1=_re3(ln2T[p])[:, :, 0:127], op=ALU.subtract),
                    deps=L(ev, tkB["mixd_rd"]))
                t_h = b.emit("dve", lambda e, p=p, pp=pp: e.tensor_tensor(
                    out=_re3(mixd)[:, :, 0:1], in0=_re3(ln2T[p])[:, :, 0:1],
                    in1=_re3(prevc[pp], k=1), op=ALU.subtract),
                    deps=L(tkB["prevc_wr"][pp]))
                tkB["prevc_rd"][pp] = t_h
                mfix = []
                if A == 0:
                    t_f0 = b.emit("dve", lambda e, p=p: e.tensor_scalar(
                        out=_re3(tmp8, k=1), in0=_re3(ln2T[p])[:, :, 0:1],
                        scalar1=msk_sb[:, 0:1], scalar2=None, op0=ALU.mult),
                        deps=[ev, t_msk])
                    t_f1 = b.emit("dve", lambda e, p=p: e.tensor_tensor(
                        out=_re3(mixd)[:, :, 1:2], in0=_re3(ln2T[p])[:, :, 1:2],
                        in1=_re3(tmp8, k=1), op=ALU.subtract),
                        deps=[t_f0, t_bulk])
                    mfix = [t_f1]
                t_m1 = b.emit("dve", lambda e: e.tensor_tensor(
                    out=tmpb, in0=mixd, in1=muMr, op=ALU.mult),
                    deps=L(t_bulk, t_h, t_mm["r"], mfix))
                t_xr = b.emit("dve", lambda e, p=p: e.tensor_tensor(
                    out=xrT[p], in0=ln2T[p], in1=tmpb, op=ALU.add),
                    deps=L(t_m1, tkB["xrT_rd"][p]))
                t_m2 = b.emit("pool", lambda e: e.tensor_tensor(
                    out=tmpk, in0=mixd, in1=muMk, op=ALU.mult),
                    deps=L(t_bulk, t_h, t_mm["k"], tkB.get("tmpk_rd"), mfix))
                t_xk = b.emit("pool", lambda e, p=p: e.tensor_tensor(
                    out=xkT[p], in0=ln2T[p], in1=tmpk, op=ALU.add),
                    deps=L(t_m2, tkB["xkT_rd"][p]))
                tkB["tmpk_rd"] = t_xk
                tkB["ln2T_rd"][p] = (t_xr, t_xk)
                tkB["mixd_rd"] = (t_m1, t_m2)
                tkB["mix_done"][A] = (t_xr, t_xk)
                if debug and A == 0:
                    d0 = b.emit("pool", lambda e: e.dma_start(
                        out=dbg["dbgB"][0:128, :], in_=ln2T[0]), deps=[ev], dma=True)
                    d1 = b.emit("pool", lambda e: e.dma_start(
                        out=dbg["dbgB"][128:256, :], in_=xrT[0]), deps=[t_xr], dma=True)
                    d2 = b.emit("pool", lambda e: e.dma_start(
                        out=dbg["dbgB"][256:384, :], in_=xkT[0]), deps=[t_xk], dma=True)
                    tkB["ln2T_rd"][p] = (t_xr, t_xk, d0)
                    tkB["mix_done"][A] = (t_xr, t_xk, d1, d2)

            def ffn_mm(ps, xT_ap, W_sb, dep_first, wd):
                tk = None
                for ci in range(8):
                    for h in range(2):
                        dl = L(dep_first, wd) if (ci == 0 and h == 0) else []
                        tk = b.emit("pe", lambda e, ci=ci, h=h, ps=ps,
                                    xT_ap=xT_ap, W_sb=W_sb: e.matmul(
                            ps[:, h * 512:(h + 1) * 512],
                            xT_ap[:, ci * 128:(ci + 1) * 128],
                            W_sb[:, ci * D + h * 512: ci * D + (h + 1) * 512],
                            start=(ci == 0), stop=(ci == 7)), deps=dl)
                return tk

            def tail(u):
                pu = u % 2
                tt = None
                t4 = None
                for ci in range(8):
                    dl = L(tkB["square"][u], tkB["ps_t2_rd"], t_id) if ci == 0 else []
                    tt = b.emit("pe", lambda e, ci=ci, pu=pu: e.transpose(
                        ps_t2[:, ci * 128:(ci + 1) * 128],
                        kctb[pu][:, ci * 128:(ci + 1) * 128], ident), deps=dl)
                    if ci == 3:
                        t4 = tt
                tkB["kctb_rd"][pu] = tt
                ev0 = b.emit("act", lambda e: e.copy(
                    kctT[:, 0:512], ps_t2[:, 0:512]),
                    deps=L(tt, tkB["kctT_rd"]))
                ev1 = b.emit("act", lambda e: e.copy(
                    kctT[:, 512:1024], ps_t2[:, 512:1024]),
                    deps=L(tkB["kctT_rd"]))
                tkB["ps_t2_rd"] = (ev0, ev1)
                tk = None
                for ci in range(8):
                    for h in range(2):
                        dl = []
                        if ci == 0 and h == 0:
                            dl = L(ev0, t_wdma["v"], tkB["y_tok"].get(u - 1))
                        elif ci == 4 and h == 0:
                            dl = [ev1]
                        tk = b.emit("pe", lambda e, ci=ci, h=h: e.matmul(
                            ps_v[:, h * 512:(h + 1) * 512],
                            kctT[:, ci * 128:(ci + 1) * 128],
                            Wv_sb[:, ci * D + h * 512: ci * D + (h + 1) * 512],
                            start=(ci == 0), stop=(ci == 7)), deps=dl)
                mv = tk
                tkB["kctT_rd"] = mv
                t_y = b.emit("dve", lambda e, pu=pu: e.scalar_tensor_tensor(
                    out=yb, in0=t_rb[pu], scalar=1.0, in1=ps_v,
                    op0=ALU.add, op1=ALU.mult),
                    deps=L(mv, tkB["tanh"][u], tkB["yb_rd"]))
                tkB["y_tok"][u] = t_y
                tkB["trb_rd"][pu] = t_y
                st = stL4[pu]
                b1 = b.emit("dve", lambda e, st=st: e.bn_stats(
                    st[:, 0:6], yb[:, 0:512]),
                    deps=L(t_y, tkB["stL4_free"][pu]))
                b2 = b.emit("dve", lambda e, st=st: e.bn_stats(
                    st[:, 6:12], yb[:, 512:1024]), deps=[b1])
                ag = b.emit("dve", lambda e, st=st: e.bn_aggr(
                    st[:, 12:14], st[:, 0:12]), deps=[b2])
                ve = b.emit("dve", lambda e, st=st: e.tensor_scalar(
                    out=st[:, 14:15], in0=st[:, 13:14], scalar1=4 * EPS,
                    scalar2=None, op0=ALU.add), deps=[ag])
                tk_rs, rstd = rsqrt(st, 15, st[:, 14:15], ve)
                dbg_toks = []
                if debug and u == 0:
                    d6 = b.emit("pool", lambda e: e.dma_start(
                        out=dbg["dbgB"][768:896, :], in_=kctT), deps=[ev0, ev1], dma=True)
                    d7 = b.emit("pool", lambda e: e.dma_start(
                        out=dbg["dbgF"][128:256, :], in_=yb), deps=[t_y], dma=True)
                    d8 = b.emit("pool", lambda e, st=st: e.dma_start(
                        out=dbg["dbgS"][:, 0:20], in_=st[:, 0:20]), deps=[tk_rs], dma=True)
                    tkB["kctT_rd"] = (mv, d6)
                    dbg_toks = [d7, d8]
                t_scr = b.emit("pool", lambda e, st=st, rstd=rstd: e.tensor_scalar(
                    out=scrB, in0=yb, scalar1=st[:, 12:13], scalar2=rstd,
                    op0=ALU.subtract, op1=ALU.mult),
                    deps=L(tk_rs, t_y, tkB["scr_rd"]))
                tkB["yb_rd"] = (t_scr,) + tuple(dbg_toks)
                tkB["stL4_free"][pu] = (t_scr,) + tuple(dbg_toks)
                t_o = b.emit("pool", lambda e, pu=pu, u=u: e.tensor_tensor(
                    out=outb[pu], in0=y1b[u % 4], in1=scrB, op=ALU.add),
                    deps=L(t_scr, tkB["outb_st"][pu]))
                tkB["scr_rd"] = t_o
                vs = max(0, u * 128 - 1)
                ve_ = min(own - 1, u * 128 + 126)
                if ve_ >= vs:
                    rs = vs - (u * 128 - 1)
                    cnt = ve_ - vs + 1
                    tkB["outb_st"][pu] = b.emit(
                        "sq", lambda e, pu=pu, vs=vs, cnt=cnt, rs=rs: e.dma_start(
                            out=out[vs:vs + cnt, :], in_=outb[pu][rs:rs + cnt, :]),
                        deps=[t_o], dma=True)
                emit_yload(u + 4, t_o)

            ln2_stats(0)
            ln2_stats(1)
            tr_mix(0)
            for t in range(nt):
                p = t % 2
                if t + 1 < nt:
                    tr_mix(t + 1)
                if t + 2 < nt:
                    ln2_stats(t + 2)
                mix_tok = tkB["mix_done"][t]
                mr = ffn_mm(ps_r, xrT[p], Wr_sb,
                            L(mix_tok[0], mix_tok[2:], tkB["tanh"].get(t - 1)),
                            t_wdma["r"])
                tkB["xrT_rd"][p] = mr
                t_t = b.emit("act", lambda e, p=p: e.activation(
                    out=t_rb[p], in_=ps_r, func=AF.Tanh, scale=0.5),
                    deps=L(mr, tkB["trb_rd"][p]))
                tkB["tanh"][t] = t_t
                mk = ffn_mm(ps_k, xkT[p], Wk_sb,
                            L(mix_tok[1], mix_tok[2:], tkB["relu"].get(t - 1)),
                            t_wdma["k"])
                tkB["xkT_rd"][p] = mk
                t_kr = b.emit("act", lambda e: e.activation(
                    out=rkb, in_=ps_k, func=AF.Relu),
                    deps=L(mk, tkB["square"].get(t - 1)))
                tkB["relu"][t] = t_kr
                t_sq = b.emit("pool", lambda e, p=p: e.tensor_tensor(
                    out=kctb[p], in0=rkb, in1=rkb, op=ALU.mult),
                    deps=L(t_kr, tkB["kctb_rd"][p]))
                tkB["square"][t] = t_sq
                if debug and t == 0:
                    d3 = b.emit("pool", lambda e: e.dma_start(
                        out=dbg["dbgF"][0:128, :], in_=t_rb[0]), deps=[t_t], dma=True)
                    d4 = b.emit("pool", lambda e: e.dma_start(
                        out=dbg["dbgB"][512:640, :], in_=rkb), deps=[t_kr], dma=True)
                    d5 = b.emit("pool", lambda e: e.dma_start(
                        out=dbg["dbgB"][640:768, :], in_=kctb[0]), deps=[t_sq], dma=True)
                    tkB["tanh"][t] = (t_t, d3)
                    tkB["square"][t] = (t_sq, d4, d5)
                if t >= 1:
                    tail(t - 1)
            tail(nt - 1)

            sq = b.st["sq"]
            b.wait("sq", [(ds, ds.n) for ds in sq.dsems if ds.n])

    return nc


def _prep_inputs(x_flat, kT, v, wrT, wkT, wvT, mur, muk, nt=33):
    own = (nt - 1) * 128
    maps = []
    for c in range(NCORES):
        lo = c * own
        sl = x_flat[lo:lo + own]
        halo = x_flat[lo - 1:lo] if (lo % 8192) != 0 else sl[0:1]
        mask = np.ones((128, 1), np.float32) if (lo % 8192) != 0 else np.zeros((128, 1), np.float32)
        pad = sl[0:127]
        xin = np.ascontiguousarray(np.concatenate([halo, sl, pad], axis=0))
        maps.append({
            "xin": xin, "kT": kT, "vv": v, "wrT": wrT, "wkT": wkT,
            "wvT": wvT, "mur": mur, "muk": muk, "msk": mask,
        })
    return maps


def _block_kT(kp):
    kT = np.asarray(kp, dtype=np.float32).T
    return np.ascontiguousarray(
        kT.reshape(8, 128, 8, 512).transpose(2, 1, 0, 3).reshape(1024, 4096))


def _block_v(v):
    v = np.asarray(v, dtype=np.float32)
    return np.ascontiguousarray(
        v.reshape(8, 4, 128, 1024).transpose(0, 2, 1, 3).reshape(1024, 4096))


def _block_w(w):
    wT = np.asarray(w, dtype=np.float32).T
    return np.ascontiguousarray(
        wT.reshape(2, 4, 128, 1024).transpose(0, 2, 1, 3).reshape(256, 4096))


def kernel(L0_attn_inputs, L0_attn_key_param_tokens, L0_attn_value_param_tokens,
           L0_ffn_W_r, L0_ffn_W_k, L0_ffn_W_v, L0_ffn_mu_r, L0_ffn_mu_k):
    x = np.ascontiguousarray(L0_attn_inputs, dtype=np.float32).reshape(-1, D)
    kT = _block_kT(L0_attn_key_param_tokens)
    v = _block_v(L0_attn_value_param_tokens)
    wrT = _block_w(L0_ffn_W_r)
    wkT = _block_w(L0_ffn_W_k)
    wvT = _block_w(L0_ffn_W_v)
    mur = np.ascontiguousarray(np.asarray(L0_ffn_mu_r, dtype=np.float32).reshape(8, 128).T)
    muk = np.ascontiguousarray(np.asarray(L0_ffn_mu_k, dtype=np.float32).reshape(8, 128).T)

    nt = 33
    nc = build_program(nt)
    in_maps = _prep_inputs(x, kT, v, wrT, wkT, wvT, mur, muk, nt)
    res = run_bass_kernel_spmd(nc, in_maps, core_ids=list(range(NCORES)))
    outs = [res.results[c]["out"] for c in range(NCORES)]
    full = np.concatenate(outs, axis=0).reshape(4, 8192, D)
    return full


# version 10
# speedup vs baseline: 1.2500x; 1.2500x over previous
import numpy as np
import concourse.bass as bass
import concourse.mybir as mybir
from concourse.bass_utils import run_bass_kernel_spmd
from contextlib import ExitStack

F32 = mybir.dt.float32
BF16 = mybir.dt.bfloat16
I32 = mybir.dt.int32
AF = mybir.ActivationFunctionType
ALU = mybir.AluOpType
AX = mybir.AxisListType

D = 1024
S = 4096
NCORES = 8
MAGIC = 0x5F3759DF
EPS = 1e-5
AF_COPY = AF.Identity


class _Sem:
    __slots__ = ("h", "step", "n")

    def __init__(self, h, step):
        self.h = h
        self.step = step
        self.n = 0


class _Stream:
    __slots__ = ("e", "csem", "dsems", "drot", "waited")

    def __init__(self, e, csem, dsems):
        self.e = e
        self.csem = csem
        self.dsems = dsems
        self.drot = 0
        self.waited = {}


class _B:

    def __init__(self, nc, ctx):
        self.nc = nc
        self.ctx = ctx
        self.st = {}
        for name, eng, n_d in (
            ("pe", nc.tensor, 0),
            ("act", nc.scalar, 0),
            ("dve", nc.vector, 0),
            ("pool", nc.gpsimd, 16),
            ("sq", nc.sync, 6),
        ):
            cs = _Sem(ctx.enter_context(nc.semaphore(name + "_c")), 1)
            ds = [_Sem(ctx.enter_context(nc.semaphore(f"{name}_d{i}")), 16)
                  for i in range(n_d)]
            self.st[name] = _Stream(eng, cs, ds)

    def emit(self, stname, fn, deps=(), dma=False):
        st = self.st[stname]
        for dep in deps:
            if dep is None:
                continue
            sem, cnt = dep
            if st.waited.get(id(sem), 0) >= cnt:
                continue
            st.e.wait_ge(sem.h, cnt * sem.step)
            st.waited[id(sem)] = cnt
        if dma:
            sem = st.dsems[st.drot]
            st.drot = (st.drot + 1) % len(st.dsems)
        else:
            sem = st.csem
        h = fn(st.e)
        h.then_inc(sem.h, sem.step)
        sem.n += 1
        return (sem, sem.n)

    def wait(self, stname, deps):
        st = self.st[stname]
        for dep in deps:
            if dep is None:
                continue
            sem, cnt = dep
            if st.waited.get(id(sem), 0) >= cnt:
                continue
            st.e.wait_ge(sem.h, cnt * sem.step)
            st.waited[id(sem)] = cnt

    def snapshot(self):
        toks = []
        for st in self.st.values():
            if st.csem.n:
                toks.append((st.csem, st.csem.n))
            for ds in st.dsems:
                if ds.n:
                    toks.append((ds, ds.n))
        return toks

    def barrier(self, toks):
        for name, st in self.st.items():
            own = {id(st.csem)} | {id(d) for d in st.dsems}
            for sem, cnt in toks:
                if id(sem) in own:
                    continue
                self.wait(name, [(sem, cnt)])


def _re3(ap, k=128):
    return ap.rearrange("p (c k) -> p c k", k=k)


def build_program(nt, debug=False):
    own = (nt - 1) * 128
    rows = nt * 128
    nc = bass.Bass("TRN2", target_bir_lowering=False)
    dbg = {}
    if debug:
        dbg["dbgB"] = nc.dram_tensor("dbgB", [8 * 128, D], BF16, kind="ExternalOutput").ap()
        dbg["dbgF"] = nc.dram_tensor("dbgF", [2 * 128, D], F32, kind="ExternalOutput").ap()
        dbg["dbgS"] = nc.dram_tensor("dbgS", [128, 32], F32, kind="ExternalOutput").ap()

    xin = nc.dram_tensor("xin", [rows, D], F32, kind="ExternalInput").ap()
    kT = nc.dram_tensor("kT", [1024, 4096], F32, kind="ExternalInput").ap()
    vv = nc.dram_tensor("vv", [1024, 4096], F32, kind="ExternalInput").ap()
    wrT = nc.dram_tensor("wrT", [256, 4096], F32, kind="ExternalInput").ap()
    wkT = nc.dram_tensor("wkT", [256, 4096], F32, kind="ExternalInput").ap()
    wvT = nc.dram_tensor("wvT", [256, 4096], F32, kind="ExternalInput").ap()
    mur = nc.dram_tensor("mur", [128, 8], F32, kind="ExternalInput").ap()
    muk = nc.dram_tensor("muk", [128, 8], F32, kind="ExternalInput").ap()
    msk = nc.dram_tensor("msk", [128, 1], F32, kind="ExternalInput").ap()
    out = nc.dram_tensor("out", [own, D], F32, kind="ExternalOutput").ap()
    x1s = nc.dram_tensor("x1s", [rows, D], F32,
                         kind="ExternalOutput" if debug else "Internal").ap()

    with ExitStack() as ctx:
        b = _B(nc, ctx)
        sb = lambda n, s, d: ctx.enter_context(nc.sbuf_tensor(n, s, d)).ap()
        pst = lambda n, s, d: ctx.enter_context(nc.psum_tensor(n, s, d)).ap()

        ident = sb("ident", [128, 128], BF16)
        magic = sb("magic", [128, 1], I32)
        mur_sb = sb("mur_sb", [128, 8], F32)
        muk_sb = sb("muk_sb", [128, 8], F32)
        msk_sb = sb("msk_sb", [128, 1], F32)

        t_id0 = b.emit("pool", lambda e: e.memset(ident, 0.0))
        t_id = b.emit(
            "pool",
            lambda e: e.affine_select(
                out=ident, in_=ident, compare_op=ALU.not_equal, fill=1.0,
                base=0, pattern=[[-1, 128]], channel_multiplier=1),
            deps=[t_id0],
        )
        t_magic = b.emit("pool", lambda e: e.memset(magic, MAGIC))
        t_mur = b.emit("sq", lambda e: e.dma_start(out=mur_sb, in_=mur), dma=True)
        t_muk = b.emit("sq", lambda e: e.dma_start(out=muk_sb, in_=muk), dma=True)
        t_msk = b.emit("sq", lambda e: e.dma_start(out=msk_sb, in_=msk), dma=True)

        def rsqrt(stt, base, v_ap, vtok):
            c0 = stt[:, base:base + 1]
            y0 = stt[:, base + 1:base + 2]
            t1 = stt[:, base + 2:base + 3]
            t2 = stt[:, base + 3:base + 4]
            yf = stt[:, base + 4:base + 5]
            tk = b.emit("dve", lambda e: e.tensor_scalar(
                out=c0.bitcast(I32), in0=v_ap.bitcast(I32), scalar1=1,
                scalar2=None, op0=ALU.logical_shift_right), deps=[vtok])
            tk = b.emit("pool", lambda e: e.tensor_tensor(
                out=y0.bitcast(I32), in0=magic, in1=c0.bitcast(I32),
                op=ALU.subtract), deps=[tk, t_magic])
            tk = b.emit("dve", lambda e: e.tensor_tensor(out=t1, in0=y0, in1=y0, op=ALU.mult), deps=[tk])
            tk = b.emit("dve", lambda e: e.tensor_tensor(out=t2, in0=t1, in1=v_ap, op=ALU.mult), deps=[tk])
            tk = b.emit("dve", lambda e: e.tensor_scalar(out=t1, in0=t2, scalar1=-0.5, scalar2=1.5, op0=ALU.mult, op1=ALU.add), deps=[tk])
            tk = b.emit("dve", lambda e: e.tensor_tensor(out=yf, in0=y0, in1=t1, op=ALU.mult), deps=[tk])
            tk = b.emit("dve", lambda e: e.tensor_tensor(out=t2, in0=yf, in1=yf, op=ALU.mult), deps=[tk])
            tk = b.emit("dve", lambda e: e.tensor_tensor(out=t1, in0=t2, in1=v_ap, op=ALU.mult), deps=[tk])
            tk = b.emit("dve", lambda e: e.tensor_scalar(out=t2, in0=t1, scalar1=-0.5, scalar2=1.5, op0=ALU.mult, op1=ALU.add), deps=[tk])
            tk = b.emit("dve", lambda e: e.tensor_tensor(out=y0, in0=yf, in1=t2, op=ALU.mult), deps=[tk])
            return tk, y0

        hmap = {}
        with ExitStack() as actx:
            def ctx_sb(c, n, s, d):
                h = c.enter_context(nc.sbuf_tensor(n, s, d))
                hmap[n] = h
                return h.ap()

            def ctx_ps(c, n, s, d):
                return c.enter_context(nc.psum_tensor(n, s, d)).ap()

            K_sb = ctx_sb(actx, "K_sb", [128, 8 * S], BF16)
            V_sb = ctx_sb(actx, "V_sb", [128, 32 * D], BF16)
            xb = [ctx_sb(actx, f"xb{i}", [128, D], F32) for i in range(3)]
            x1b = ctx_sb(actx, "x1b", [128, D], F32)
            ln1b = [ctx_sb(actx, f"ln1b{i}", [128, D], BF16) for i in range(2)]
            ln1T = [ctx_sb(actx, f"ln1T{i}", [128, D], BF16) for i in range(2)]
            gbf = [ctx_sb(actx, f"gbf{i}", [128, 512], BF16) for i in range(2)]
            gTb = [ctx_sb(actx, f"gTb{i}", [128, 512], BF16) for i in range(2)]
            ao = ctx_sb(actx, "ao", [128, D], F32)
            scr = ctx_sb(actx, "scr", [128, D], F32)
            junk = ctx_sb(actx, "junk", [128, D], F32)
            stats = [ctx_sb(actx, f"stats{i}", [128, 32], F32) for i in range(2)]

            ps_s = [ctx_ps(actx, f"ps_s{i}", [128, 512], F32) for i in range(2)]
            ps_t = [ctx_ps(actx, f"ps_t{i}", [128, 1024], BF16) for i in range(2)]
            ps_lt = [ctx_ps(actx, f"ps_lt{i}", [128, 1024], BF16) for i in range(2)]
            ps_o = ctx_ps(actx, "ps_o", [128, 1024], F32)

            t_kdma = [None] * 8
            t_vdma = [None] * 8
            for j in range(8):
                t_kdma[j] = b.emit(
                    "pool",
                    lambda e, j=j: e.dma_start(
                        out=K_sb[:, j * 4096:(j + 1) * 4096],
                        in_=kT[j * 128:(j + 1) * 128, :]),
                    dma=True)
                t_vdma[j] = b.emit(
                    "pool", lambda e, j=j: e.dma_start(
                        out=V_sb[:, j * 4096:(j + 1) * 4096],
                        in_=vv[j * 128:(j + 1) * 128, :]),
                    dma=True)

            t_ld = {}

            def emit_load(t, dep):
                if t >= nt:
                    return
                p3 = t % 3
                t_ld[t] = b.emit(
                    "sq",
                    lambda e: e.dma_start(out=xb[p3], in_=xin[t * 128:(t + 1) * 128, :]),
                    deps=[dep], dma=True)

            emit_load(0, None)
            emit_load(1, None)
            emit_load(2, None)

            tok = {
                "ln1b_rd": [None, None],
                "ln1T_rd": [None, None],
                "pslt_rd": [None, None],
                "pss_rd": [None, None],
                "gbf_rd": [None, None],
                "pst_rd": [None, None],
                "gTb_rd": [None, None],
                "ao_wr": None,
                "ao_dve_rd": None,
                "tmp_wr": None,
                "x1add": {},
                "store": None,
                "store_by_tile": {},
                "mm2_last": None,
                "last_mm1": None,
                "stats_rd_act": [None, None],
            }

            pdve = {}

            def pass_a_P_dve(t):
                p = t % 2
                p3 = t % 3
                stt = stats[p]
                d_x = t_ld[t]
                tk_r = b.emit("dve", lambda e: e.reduce_sum(
                    out=stt[:, 0:1], in_=xb[p3], axis=AX.X), deps=[d_x])
                tk_m = b.emit("dve", lambda e: e.tensor_scalar(
                    out=stt[:, 1:2], in0=stt[:, 0:1], scalar1=1.0 / D,
                    scalar2=None, op0=ALU.mult), deps=[tk_r])
                tk_v = b.emit("dve", lambda e: e.scalar_tensor_tensor(
                    out=junk, in0=xb[p3], scalar=stt[:, 1:2], in1=xb[p3],
                    op0=ALU.subtract, op1=ALU.mult, accum_out=stt[:, 2:3]),
                    deps=[tk_m, tok.get("junk_wr")])
                tok["junk_wr"] = tk_v
                tk_ve = b.emit("dve", lambda e: e.tensor_scalar(
                    out=stt[:, 3:4], in0=stt[:, 2:3], scalar1=1.0 / D,
                    scalar2=EPS, op0=ALU.mult, op1=ALU.add), deps=[tk_v])
                tk_rs, rstd = rsqrt(stt, 4, stt[:, 3:4], tk_ve)
                tk_b = b.emit("dve", lambda e: e.scalar_tensor_tensor(
                    out=stt[:, 9:10], in0=stt[:, 1:2], scalar=-1.0, in1=rstd,
                    op0=ALU.mult, op1=ALU.mult), deps=[tk_rs])
                pdve[t] = (tk_b, rstd)

            def pass_a_P_act(t):
                p = t % 2
                p3 = t % 3
                stt = stats[p]
                tk_b, rstd = pdve[t]
                t_n = b.emit("act", lambda e: e.activation(
                    out=ln1b[p], in_=xb[p3], func=AF_COPY,
                    bias=stt[:, 9:10], scale=rstd),
                    deps=[tk_b, tok["ln1b_rd"][p]])
                tt = None
                for ci in range(8):
                    dl = [t_n, tok["pslt_rd"][p]] if ci == 0 else []
                    tt = b.emit("pe", lambda e, ci=ci: e.transpose(
                        ps_lt[p][:, ci * 128:(ci + 1) * 128],
                        ln1b[p][:, ci * 128:(ci + 1) * 128], ident),
                        deps=dl + ([t_id] if ci == 0 else []))
                tok["ln1b_rd"][p] = tt
                ev = b.emit("act", lambda e: e.copy(ln1T[p], ps_lt[p]),
                            deps=[tt, tok["ln1T_rd"][p]])
                tok["pslt_rd"][p] = ev
                return ev

            def pass_a_C(t, ev_ln1T):
                p = t % 2
                stt = stats[p]
                mm1 = {}
                gel = {}
                g2tok = {}
                tr = {}
                evg = {}
                mm2tok = None

                def do_mm1(j):
                    par = j % 2
                    tk = None
                    for d in range(8):
                        dl = []
                        if d == 0:
                            dl = [ev_ln1T, t_kdma[j], tok["pss_rd"][par]]
                        tk = b.emit("pe", lambda e, d=d, j=j, par=par: e.matmul(
                            ps_s[par],
                            ln1T[p][:, d * 128:(d + 1) * 128],
                            K_sb[:, j * 4096 + d * 512: j * 4096 + (d + 1) * 512],
                            start=(d == 0), stop=(d == 7)), deps=dl)
                    mm1[j] = tk

                def do_gelu(j):
                    par = j % 2
                    gel[j] = b.emit("act", lambda e, par=par: e.activation(
                        out=gbf[par], in_=ps_s[par], func=AF.Gelu),
                        deps=[mm1[j]] + list(tok["gbf_rd"][par] or ()))
                    tok["pss_rd"][par] = gel[j]

                def do_g2(j):
                    par = j % 2
                    g2tok[j] = b.emit("dve", lambda e, j=j, par=par: e.scalar_tensor_tensor(
                        out=junk[:, 0:512], in0=gbf[par], scalar=0.0,
                        in1=gbf[par], op0=ALU.bypass, op1=ALU.mult,
                        accum_out=stt[:, 21 + j:22 + j]),
                        deps=[gel[j], tok.get("junk_wr")])
                    tok["junk_wr"] = g2tok[j]

                def do_tr(j):
                    par = j % 2
                    tk = None
                    for q in range(4):
                        dl = [gel[j], tok["pst_rd"][par]] if q == 0 else []
                        tk = b.emit("pe", lambda e, q=q, par=par: e.transpose(
                            ps_t[par][:, q * 128:(q + 1) * 128],
                            gbf[par][:, q * 128:(q + 1) * 128], ident), deps=dl)
                    tr[j] = tk

                def do_evg(j):
                    par = j % 2
                    evg[j] = b.emit("act", lambda e, par=par: e.copy(
                        gTb[par], ps_t[par][:, 0:512]),
                        deps=[tr[j], tok["gTb_rd"][par]])
                    tok["pst_rd"][par] = evg[j]

                def do_mm2(j):
                    nonlocal mm2tok
                    par = j % 2
                    tk = None
                    for q in range(4):
                        for h in range(2):
                            dl = []
                            if q == 0 and h == 0:
                                dl = [evg[j], t_vdma[j], tok["ao_wr"]]
                            st_flag = (j == 0 and q == 0)
                            sp_flag = (j == 7 and q == 3)
                            tk = b.emit("pe", lambda e, j=j, q=q, h=h, par=par,
                                        st_flag=st_flag, sp_flag=sp_flag: e.matmul(
                                ps_o[:, h * 512:(h + 1) * 512],
                                gTb[par][:, q * 128:(q + 1) * 128],
                                V_sb[:, (4 * j + q) * D + h * 512:(4 * j + q) * D + (h + 1) * 512],
                                start=st_flag, stop=sp_flag), deps=dl)
                    mm2[j] = tk
                    mm2tok = tk

                mm2 = {}
                do_mm1(0)
                do_gelu(0)
                do_mm1(1)
                do_gelu(1)
                if t + 1 < nt:
                    pass_a_P_dve(t + 1)
                do_g2(0)
                do_g2(1)
                if t > 0:
                    pass_a_E_rest(t - 1, g2_of[t - 1], ao_of[t - 1])
                    emit_load(t + 2, tok["x1add"].get(t - 1))
                for j in range(8):
                    do_tr(j)
                    do_evg(j)
                    tok["gbf_rd"][j % 2] = (tr[j], g2tok[j])
                    if j + 2 < 8:
                        do_mm1(j + 2)
                    do_mm2(j)
                    tok["gTb_rd"][j % 2] = mm2[j]
                    if j + 2 < 8:
                        do_gelu(j + 2)
                        do_g2(j + 2)
                tok["mm2_last"] = mm2tok
                tok["ln1T_rd"][p] = mm1[7]
                tok["last_mm1"] = mm1[7]
                ev_next = pass_a_P_act(t + 1) if t + 1 < nt else None
                return g2tok[7], mm2tok, ev_next

            def pass_a_ao(t, mm2tok):
                t_ao = b.emit("act", lambda e: e.copy(ao, ps_o),
                              deps=[mm2tok, tok["ao_dve_rd"]])
                tok["ao_wr"] = t_ao
                return t_ao

            def pass_a_E_rest(t, g2last, t_ao):
                p = t % 2
                p3 = t % 3
                stt = stats[p]
                t_sg = b.emit("dve", lambda e: e.reduce_sum(
                    out=stt[:, 29:30], in_=stt[:, 21:29], axis=AX.X),
                    deps=[g2last])
                tk_r = b.emit("dve", lambda e: e.reduce_sum(
                    out=stt[:, 10:11], in_=ao, axis=AX.X), deps=[t_ao])
                tk_m = b.emit("dve", lambda e: e.tensor_scalar(
                    out=stt[:, 11:12], in0=stt[:, 10:11], scalar1=1.0 / D,
                    scalar2=None, op0=ALU.mult), deps=[tk_r])
                tk_v = b.emit("dve", lambda e: e.scalar_tensor_tensor(
                    out=junk, in0=ao, scalar=stt[:, 11:12], in1=ao,
                    op0=ALU.subtract, op1=ALU.mult, accum_out=stt[:, 12:13]),
                    deps=[tk_m, tok.get("junk_wr")])
                tok["junk_wr"] = tk_v
                tok["ao_dve_rd"] = tk_v
                tk_vs = b.emit("dve", lambda e: e.tensor_scalar(
                    out=stt[:, 13:14], in0=stt[:, 12:13], scalar1=1.0 / D,
                    scalar2=None, op0=ALU.mult), deps=[tk_v])
                tk_ve = b.emit("dve", lambda e: e.scalar_tensor_tensor(
                    out=stt[:, 14:15], in0=stt[:, 29:30], scalar=EPS / S,
                    in1=stt[:, 13:14], op0=ALU.mult, op1=ALU.add),
                    deps=[tk_vs, t_sg])
                tk_rs, rstd = rsqrt(stt, 15, stt[:, 14:15], tk_ve)
                t_tmp = b.emit("dve", lambda e: e.tensor_scalar(
                    out=scr, in0=ao, scalar1=stt[:, 11:12], scalar2=rstd,
                    op0=ALU.subtract, op1=ALU.mult),
                    deps=[tk_rs, tok.get("x1add_prev_scr")])
                t_x1 = b.emit("dve", lambda e: e.tensor_tensor(
                    out=x1b, in0=xb[p3], in1=scr, op=ALU.add),
                    deps=[t_tmp, tok["store"]])
                tok["x1add_prev_scr"] = t_x1
                tok["x1add"][t] = t_x1
                tok["store"] = b.emit("sq", lambda e: e.dma_start(
                    out=x1s[t * 128:(t + 1) * 128, :], in_=x1b),
                    deps=[t_x1], dma=True)
                tok["store_by_tile"][t] = tok["store"]

            tok["x1add_prev_scr"] = None
            pass_a_P_dve(0)
            ev_next = pass_a_P_act(0)
            g2_of = {}
            ao_of = {}
            mm2_prev = None
            for t in range(nt):
                if t > 0:
                    ao_of[t - 1] = pass_a_ao(t - 1, mm2_prev)
                g2_of[t], mm2_prev, ev_next = pass_a_C(t, ev_next)
            ao_of[nt - 1] = pass_a_ao(nt - 1, mm2_prev)
            pass_a_E_rest(nt - 1, g2_of[nt - 1], ao_of[nt - 1])

            bar = b.snapshot()

        def L(*xs):
            out_ = []
            for x in xs:
                if x is None:
                    continue
                if isinstance(x, tuple) and len(x) == 2 and isinstance(x[0], _Sem):
                    out_.append(x)
                elif isinstance(x, (list, tuple)):
                    out_.extend(L(*x))
                else:
                    out_.append(x)
            return out_

        with ExitStack() as bctx:
            hB = {}

            def bsb(n, s, d):
                h = bctx.enter_context(nc.sbuf_tensor(n, s, d))
                hB[n] = h
                return h.ap()

            def bps(n, s, d):
                return bctx.enter_context(nc.psum_tensor(n, s, d)).ap()

            Wk_sb = bsb("Wk_sb", [128, 8 * D], BF16)
            Wv_sb = bsb("Wv_sb", [128, 8 * D], BF16)
            Wr_sb = bsb("Wr_sb", [128, 8 * D], BF16)
            y1b = [bsb(f"y1b{i}", [128, D], F32) for i in range(4)]
            k_addr = nc.lookup_mloc(hmap["K_sb"]).addr
            assert nc.lookup_mloc(hB["Wk_sb"]).addr == k_addr, "alloc layout"
            assert nc.lookup_mloc(hB["y1b3"]).addr + 4096 <= k_addr + 65536, \
                "y1 ring must stay inside K_sb's span"

            lm = tok["last_mm1"]
            t_wdma = {"r": [], "k": [], "v": []}
            for nm, dram, dst in (("k", wkT, Wk_sb), ("v", wvT, Wv_sb),
                                  ("r", wrT, Wr_sb)):
                for h in range(2):
                    t_wdma[nm].append(b.emit(
                        "pool",
                        lambda e, dram=dram, dst=dst, h=h: e.dma_start(
                            out=dst[:, h * 4096:(h + 1) * 4096],
                            in_=dram[h * 128:(h + 1) * 128, :]),
                        deps=[lm], dma=True))

            t_yld = {}

            def emit_yload(t, extra):
                if t >= nt:
                    return
                t_yld[t] = b.emit(
                    "sq",
                    lambda e, t=t: e.dma_start(
                        out=y1b[t % 4], in_=x1s[t * 128:(t + 1) * 128, :]),
                    deps=L(extra), dma=True)

            b.barrier(bar)
            b.wait("sq", [(ds, ds.n) for ds in b.st["sq"].dsems if ds.n])

            for t0 in range(4):
                emit_yload(t0, None)

            ln2b = [bsb(f"ln2b{i}", [128, D], BF16) for i in range(2)]
            ln2T = [bsb(f"ln2T{i}", [128, D], BF16) for i in range(2)]
            xrT = [bsb(f"xrT{i}", [128, D], BF16) for i in range(2)]
            xkT = [bsb(f"xkT{i}", [128, D], BF16) for i in range(2)]
            mixd = bsb("mixd", [128, D], BF16)
            muMr = bsb("muMr", [128, D], BF16)
            muMk = bsb("muMk", [128, D], BF16)
            tmpb = bsb("tmpb", [128, D], BF16)
            tmpk = bsb("tmpk", [128, D], BF16)
            t_rb = [bsb(f"t_rb{i}", [128, D], F32) for i in range(2)]
            rkb = bsb("rkb", [128, D], BF16)
            kctb = [bsb(f"kctb{i}", [128, D], BF16) for i in range(2)]
            kctT = bsb("kctT", [128, D], BF16)
            yb = bsb("yb", [128, D], F32)
            scrB = bsb("scrB", [128, D], F32)
            outb = [bsb(f"outb{i}", [128, D], F32) for i in range(2)]
            stL2 = [bsb(f"stL2{i}", [128, 32], F32) for i in range(2)]
            stL4 = [bsb(f"stL4{i}", [128, 32], F32) for i in range(2)]
            prevc = [bsb(f"prevc{i}", [128, 8], BF16) for i in range(2)]
            tmp8 = bsb("tmp8", [128, 8], BF16)

            ps_r = bps("ps_r", [128, 1024], F32)
            ps_k = bps("ps_k", [128, 1024], F32)
            ps_v = bps("ps_v", [128, 1024], F32)
            ps_t1 = bps("ps_tb1", [128, 1024], BF16)
            ps_t2 = bps("ps_tb2", [128, 1024], BF16)

            t_z0 = b.emit("pool", lambda e: e.memset(muMr, 0.0))
            t_z1 = b.emit("pool", lambda e: e.memset(muMk, 0.0))
            t_mm = {}
            for nm, dst, src, tz, tmu in (("r", muMr, mur_sb, t_z0, t_mur),
                                          ("k", muMk, muk_sb, t_z1, t_muk)):
                tk = None
                for ci in range(8):
                    dl = [tz, tmu] if ci == 0 else []
                    tk = b.emit("dve", lambda e, dst=dst, src=src, ci=ci: e.tensor_scalar(
                        out=dst[:, ci * 128:(ci + 1) * 128],
                        in0=dst[:, ci * 128:(ci + 1) * 128],
                        scalar1=src[:, ci:ci + 1], scalar2=-1.0,
                        op0=ALU.add, op1=ALU.add), deps=dl)
                t_mm[nm] = tk
            t_pc0 = b.emit("pool", lambda e: e.memset(prevc[0], 0.0))
            t_pc1 = b.emit("pool", lambda e: e.memset(prevc[1], 0.0))

            tkB = {
                "ln2b_rd": [None, None],
                "ln2b_wr": {},
                "ln2T_rd": [None, None],
                "ps_t1_rd": None, "ps_t2_rd": None,
                "mix_done": {},
                "mixd_rd": None,
                "xrT_rd": [None, None], "xkT_rd": [None, None],
                "tanh": {}, "relu": {}, "square": {},
                "trb_rd": [None, None],
                "kctb_rd": [None, None],
                "kctT_rd": None,
                "y_tok": {},
                "yb_rd": None, "scr_rd": None,
                "outb_st": [None, None],
                "prevc_rd": [t_pc0, t_pc1],
                "prevc_wr": [t_pc0, t_pc1],
                "stL2_free": [None, None],
                "stL4_free": [None, None],
            }

            def ln2_stats(t):
                p = t % 2
                st = stL2[p]
                b1 = b.emit("dve", lambda e, st=st, t=t: e.bn_stats(
                    st[:, 0:6], y1b[t % 4][:, 0:512]),
                    deps=L(t_yld[t], tkB["stL2_free"][p]))
                b2 = b.emit("dve", lambda e, st=st, t=t: e.bn_stats(
                    st[:, 6:12], y1b[t % 4][:, 512:1024]), deps=[b1])
                ag = b.emit("dve", lambda e, st=st: e.bn_aggr(
                    st[:, 12:14], st[:, 0:12]), deps=[b2])
                ve = b.emit("dve", lambda e, st=st: e.tensor_scalar(
                    out=st[:, 14:15], in0=st[:, 13:14], scalar1=EPS,
                    scalar2=None, op0=ALU.add), deps=[ag])
                tk_rs, rstd = rsqrt(st, 15, st[:, 14:15], ve)
                tk_b = b.emit("dve", lambda e, st=st: e.scalar_tensor_tensor(
                    out=st[:, 20:21], in0=st[:, 12:13], scalar=-1.0, in1=rstd,
                    op0=ALU.mult, op1=ALU.mult), deps=[tk_rs])
                t_n = b.emit("act", lambda e, st=st, p=p, t=t, rstd=rstd: e.activation(
                    out=ln2b[p], in_=y1b[t % 4], func=AF_COPY,
                    bias=st[:, 20:21], scale=rstd),
                    deps=L(tk_b, tkB["ln2b_rd"][p]))
                tkB["stL2_free"][p] = t_n
                tkB["ln2b_wr"][t] = t_n

            def tr_mix(A):
                p = A % 2
                tt = None
                for ci in range(8):
                    dl = L(tkB["ln2b_wr"][A], tkB["ps_t1_rd"], t_id) if ci == 0 else []
                    tt = b.emit("pe", lambda e, ci=ci, p=p: e.transpose(
                        ps_t1[:, ci * 128:(ci + 1) * 128],
                        ln2b[p][:, ci * 128:(ci + 1) * 128], ident), deps=dl)
                tkB["ln2b_rd"][p] = tt
                ev = b.emit("act", lambda e, p=p: e.copy(ln2T[p], ps_t1),
                            deps=L(tt, tkB["ln2T_rd"][p]))
                tkB["ps_t1_rd"] = ev
                l3 = _re3(ln2T[p])
                m3 = _re3(mixd)
                pcw = b.emit("dve", lambda e, p=p: e.tensor_copy(
                    out=_re3(prevc[p], k=1), in_=_re3(ln2T[p])[:, :, 127:128]),
                    deps=L(ev, tkB["prevc_rd"][p]))
                tkB["prevc_wr"][p] = pcw
                pp = (A - 1) % 2
                t_bulk = b.emit("dve", lambda e, p=p: e.tensor_tensor(
                    out=_re3(mixd)[:, :, 1:128], in0=_re3(ln2T[p])[:, :, 1:128],
                    in1=_re3(ln2T[p])[:, :, 0:127], op=ALU.subtract),
                    deps=L(ev, tkB["mixd_rd"]))
                t_h = b.emit("dve", lambda e, p=p, pp=pp: e.tensor_tensor(
                    out=_re3(mixd)[:, :, 0:1], in0=_re3(ln2T[p])[:, :, 0:1],
                    in1=_re3(prevc[pp], k=1), op=ALU.subtract),
                    deps=L(tkB["prevc_wr"][pp]))
                tkB["prevc_rd"][pp] = t_h
                mfix = []
                if A == 0:
                    t_f0 = b.emit("dve", lambda e, p=p: e.tensor_scalar(
                        out=_re3(tmp8, k=1), in0=_re3(ln2T[p])[:, :, 0:1],
                        scalar1=msk_sb[:, 0:1], scalar2=None, op0=ALU.mult),
                        deps=[ev, t_msk])
                    t_f1 = b.emit("dve", lambda e, p=p: e.tensor_tensor(
                        out=_re3(mixd)[:, :, 1:2], in0=_re3(ln2T[p])[:, :, 1:2],
                        in1=_re3(tmp8, k=1), op=ALU.subtract),
                        deps=[t_f0, t_bulk])
                    mfix = [t_f1]
                t_m1 = b.emit("dve", lambda e: e.tensor_tensor(
                    out=tmpb, in0=mixd, in1=muMr, op=ALU.mult),
                    deps=L(t_bulk, t_h, t_mm["r"], mfix))
                t_xr = b.emit("dve", lambda e, p=p: e.tensor_tensor(
                    out=xrT[p], in0=ln2T[p], in1=tmpb, op=ALU.add),
                    deps=L(t_m1, tkB["xrT_rd"][p]))
                t_m2 = b.emit("dve", lambda e: e.tensor_tensor(
                    out=tmpk, in0=mixd, in1=muMk, op=ALU.mult),
                    deps=L(t_bulk, t_h, t_mm["k"], tkB.get("tmpk_rd"), mfix))
                t_xk = b.emit("dve", lambda e, p=p: e.tensor_tensor(
                    out=xkT[p], in0=ln2T[p], in1=tmpk, op=ALU.add),
                    deps=L(t_m2, tkB["xkT_rd"][p]))
                tkB["tmpk_rd"] = t_xk
                tkB["ln2T_rd"][p] = (t_xr, t_xk)
                tkB["mixd_rd"] = (t_m1, t_m2)
                tkB["mix_done"][A] = (t_xr, t_xk)
                if debug and A == 0:
                    d0 = b.emit("pool", lambda e: e.dma_start(
                        out=dbg["dbgB"][0:128, :], in_=ln2T[0]), deps=[ev], dma=True)
                    d1 = b.emit("pool", lambda e: e.dma_start(
                        out=dbg["dbgB"][128:256, :], in_=xrT[0]), deps=[t_xr], dma=True)
                    d2 = b.emit("pool", lambda e: e.dma_start(
                        out=dbg["dbgB"][256:384, :], in_=xkT[0]), deps=[t_xk], dma=True)
                    tkB["ln2T_rd"][p] = (t_xr, t_xk, d0)
                    tkB["mix_done"][A] = (t_xr, t_xk, d1, d2)

            def ffn_mm(ps, xT_ap, W_sb, dep_first, wd):
                tk = None
                for ci in range(8):
                    for h in range(2):
                        dl = L(dep_first, wd) if (ci == 0 and h == 0) else []
                        tk = b.emit("pe", lambda e, ci=ci, h=h, ps=ps,
                                    xT_ap=xT_ap, W_sb=W_sb: e.matmul(
                            ps[:, h * 512:(h + 1) * 512],
                            xT_ap[:, ci * 128:(ci + 1) * 128],
                            W_sb[:, ci * D + h * 512: ci * D + (h + 1) * 512],
                            start=(ci == 0), stop=(ci == 7)), deps=dl)
                return tk

            def tail(u):
                pu = u % 2
                tt = None
                t4 = None
                for ci in range(8):
                    dl = L(tkB["square"][u], tkB["ps_t2_rd"], t_id) if ci == 0 else []
                    tt = b.emit("pe", lambda e, ci=ci, pu=pu: e.transpose(
                        ps_t2[:, ci * 128:(ci + 1) * 128],
                        kctb[pu][:, ci * 128:(ci + 1) * 128], ident), deps=dl)
                    if ci == 3:
                        t4 = tt
                tkB["kctb_rd"][pu] = tt
                ev0 = b.emit("act", lambda e: e.copy(
                    kctT[:, 0:512], ps_t2[:, 0:512]),
                    deps=L(tt, tkB["kctT_rd"]))
                ev1 = b.emit("act", lambda e: e.copy(
                    kctT[:, 512:1024], ps_t2[:, 512:1024]),
                    deps=L(tkB["kctT_rd"]))
                tkB["ps_t2_rd"] = (ev0, ev1)
                tk = None
                for ci in range(8):
                    for h in range(2):
                        dl = []
                        if ci == 0 and h == 0:
                            dl = L(ev0, t_wdma["v"], tkB["y_tok"].get(u - 1))
                        elif ci == 4 and h == 0:
                            dl = [ev1]
                        tk = b.emit("pe", lambda e, ci=ci, h=h: e.matmul(
                            ps_v[:, h * 512:(h + 1) * 512],
                            kctT[:, ci * 128:(ci + 1) * 128],
                            Wv_sb[:, ci * D + h * 512: ci * D + (h + 1) * 512],
                            start=(ci == 0), stop=(ci == 7)), deps=dl)
                mv = tk
                tkB["kctT_rd"] = mv
                t_y = b.emit("dve", lambda e, pu=pu: e.scalar_tensor_tensor(
                    out=yb, in0=t_rb[pu], scalar=1.0, in1=ps_v,
                    op0=ALU.add, op1=ALU.mult),
                    deps=L(mv, tkB["tanh"][u], tkB["yb_rd"]))
                tkB["y_tok"][u] = t_y
                tkB["trb_rd"][pu] = t_y
                st = stL4[pu]
                b1 = b.emit("dve", lambda e, st=st: e.bn_stats(
                    st[:, 0:6], yb[:, 0:512]),
                    deps=L(t_y, tkB["stL4_free"][pu]))
                b2 = b.emit("dve", lambda e, st=st: e.bn_stats(
                    st[:, 6:12], yb[:, 512:1024]), deps=[b1])
                ag = b.emit("dve", lambda e, st=st: e.bn_aggr(
                    st[:, 12:14], st[:, 0:12]), deps=[b2])
                ve = b.emit("dve", lambda e, st=st: e.tensor_scalar(
                    out=st[:, 14:15], in0=st[:, 13:14], scalar1=4 * EPS,
                    scalar2=None, op0=ALU.add), deps=[ag])
                tk_rs, rstd = rsqrt(st, 15, st[:, 14:15], ve)
                dbg_toks = []
                if debug and u == 0:
                    d6 = b.emit("pool", lambda e: e.dma_start(
                        out=dbg["dbgB"][768:896, :], in_=kctT), deps=[ev0, ev1], dma=True)
                    d7 = b.emit("pool", lambda e: e.dma_start(
                        out=dbg["dbgF"][128:256, :], in_=yb), deps=[t_y], dma=True)
                    d8 = b.emit("pool", lambda e, st=st: e.dma_start(
                        out=dbg["dbgS"][:, 0:20], in_=st[:, 0:20]), deps=[tk_rs], dma=True)
                    tkB["kctT_rd"] = (mv, d6)
                    dbg_toks = [d7, d8]
                t_scr = b.emit("dve", lambda e, st=st, rstd=rstd: e.tensor_scalar(
                    out=scrB, in0=yb, scalar1=st[:, 12:13], scalar2=rstd,
                    op0=ALU.subtract, op1=ALU.mult),
                    deps=L(tk_rs, t_y, tkB["scr_rd"]))
                tkB["yb_rd"] = (t_scr,) + tuple(dbg_toks)
                tkB["stL4_free"][pu] = (t_scr,) + tuple(dbg_toks)
                t_o = b.emit("dve", lambda e, pu=pu, u=u: e.tensor_tensor(
                    out=outb[pu], in0=y1b[u % 4], in1=scrB, op=ALU.add),
                    deps=L(t_scr, tkB["outb_st"][pu]))
                tkB["scr_rd"] = t_o
                vs = max(0, u * 128 - 1)
                ve_ = min(own - 1, u * 128 + 126)
                if ve_ >= vs:
                    rs = vs - (u * 128 - 1)
                    cnt = ve_ - vs + 1
                    tkB["outb_st"][pu] = b.emit(
                        "sq", lambda e, pu=pu, vs=vs, cnt=cnt, rs=rs: e.dma_start(
                            out=out[vs:vs + cnt, :], in_=outb[pu][rs:rs + cnt, :]),
                        deps=[t_o], dma=True)
                emit_yload(u + 4, t_o)

            ln2_stats(0)
            ln2_stats(1)
            tr_mix(0)
            for t in range(nt):
                p = t % 2
                if t + 1 < nt:
                    tr_mix(t + 1)
                if t + 2 < nt:
                    ln2_stats(t + 2)
                mix_tok = tkB["mix_done"][t]
                mr = ffn_mm(ps_r, xrT[p], Wr_sb,
                            L(mix_tok[0], mix_tok[2:], tkB["tanh"].get(t - 1)),
                            t_wdma["r"])
                tkB["xrT_rd"][p] = mr
                t_t = b.emit("act", lambda e, p=p: e.activation(
                    out=t_rb[p], in_=ps_r, func=AF.Tanh, scale=0.5),
                    deps=L(mr, tkB["trb_rd"][p]))
                tkB["tanh"][t] = t_t
                if t >= 1:
                    tail(t - 1)
                mk = ffn_mm(ps_k, xkT[p], Wk_sb,
                            L(mix_tok[1], mix_tok[2:], tkB["relu"].get(t - 1)),
                            t_wdma["k"])
                tkB["xkT_rd"][p] = mk
                t_kr = b.emit("act", lambda e: e.activation(
                    out=rkb, in_=ps_k, func=AF.Relu),
                    deps=L(mk, tkB["square"].get(t - 1)))
                tkB["relu"][t] = t_kr
                t_sq = b.emit("act", lambda e, p=p: e.activation(
                    out=kctb[p], in_=rkb, func=AF.Square),
                    deps=L(t_kr, tkB["kctb_rd"][p]))
                tkB["square"][t] = t_sq
                if debug and t == 0:
                    d3 = b.emit("pool", lambda e: e.dma_start(
                        out=dbg["dbgF"][0:128, :], in_=t_rb[0]), deps=[t_t], dma=True)
                    d4 = b.emit("pool", lambda e: e.dma_start(
                        out=dbg["dbgB"][512:640, :], in_=rkb), deps=[t_kr], dma=True)
                    d5 = b.emit("pool", lambda e: e.dma_start(
                        out=dbg["dbgB"][640:768, :], in_=kctb[0]), deps=[t_sq], dma=True)
                    tkB["tanh"][t] = (t_t, d3)
                    tkB["square"][t] = (t_sq, d4, d5)
            tail(nt - 1)

            sq = b.st["sq"]
            b.wait("sq", [(ds, ds.n) for ds in sq.dsems if ds.n])

    return nc


def _prep_inputs(x_flat, kT, v, wrT, wkT, wvT, mur, muk, nt=33):
    own = (nt - 1) * 128
    maps = []
    for c in range(NCORES):
        lo = c * own
        sl = x_flat[lo:lo + own]
        halo = x_flat[lo - 1:lo] if (lo % 8192) != 0 else sl[0:1]
        mask = np.ones((128, 1), np.float32) if (lo % 8192) != 0 else np.zeros((128, 1), np.float32)
        pad = sl[0:127]
        xin = np.ascontiguousarray(np.concatenate([halo, sl, pad], axis=0))
        maps.append({
            "xin": xin, "kT": kT, "vv": v, "wrT": wrT, "wkT": wkT,
            "wvT": wvT, "mur": mur, "muk": muk, "msk": mask,
        })
    return maps


def _block_kT(kp):
    kT = np.asarray(kp, dtype=np.float32).T
    return np.ascontiguousarray(
        kT.reshape(8, 128, 8, 512).transpose(2, 1, 0, 3).reshape(1024, 4096))


def _block_v(v):
    v = np.asarray(v, dtype=np.float32)
    return np.ascontiguousarray(
        v.reshape(8, 4, 128, 1024).transpose(0, 2, 1, 3).reshape(1024, 4096))


def _block_w(w):
    wT = np.asarray(w, dtype=np.float32).T
    return np.ascontiguousarray(
        wT.reshape(2, 4, 128, 1024).transpose(0, 2, 1, 3).reshape(256, 4096))


def kernel(L0_attn_inputs, L0_attn_key_param_tokens, L0_attn_value_param_tokens,
           L0_ffn_W_r, L0_ffn_W_k, L0_ffn_W_v, L0_ffn_mu_r, L0_ffn_mu_k):
    x = np.ascontiguousarray(L0_attn_inputs, dtype=np.float32).reshape(-1, D)
    kT = _block_kT(L0_attn_key_param_tokens)
    v = _block_v(L0_attn_value_param_tokens)
    wrT = _block_w(L0_ffn_W_r)
    wkT = _block_w(L0_ffn_W_k)
    wvT = _block_w(L0_ffn_W_v)
    mur = np.ascontiguousarray(np.asarray(L0_ffn_mu_r, dtype=np.float32).reshape(8, 128).T)
    muk = np.ascontiguousarray(np.asarray(L0_ffn_mu_k, dtype=np.float32).reshape(8, 128).T)

    nt = 33
    nc = build_program(nt)
    in_maps = _prep_inputs(x, kT, v, wrT, wkT, wvT, mur, muk, nt)
    res = run_bass_kernel_spmd(nc, in_maps, core_ids=list(range(NCORES)))
    outs = [res.results[c]["out"] for c in range(NCORES)]
    full = np.concatenate(outs, axis=0).reshape(4, 8192, D)
    return full
